# version 09
# baseline (speedup 1.0000x reference)
import math
import contextlib
import numpy as np

import concourse.bass as bass
import concourse.mybir as mybir
from concourse.bass_utils import run_bass_kernel_spmd

F32 = mybir.dt.float32
BF16 = mybir.dt.bfloat16
U32 = mybir.dt.uint32
U8 = mybir.dt.uint8
AF = mybir.ActivationFunctionType
ALU = mybir.AluOpType
AX = mybir.AxisListType

B_TOT = 4
D = 1024
T = 4096
NH = 16
HD = 64
DC = 128
DR = 32
NTB = 32
SCALE = 1.0 / math.sqrt(HD + DR)
CSH = 12.0
EPS = 1e-5

PERM = [[0, 3, 1, 2], [1, 2, 0, 3]]
SLOT = [[0, 2, 3, 1], [2, 0, 1, 3]]

CHANNELS = ("pe", "act", "dve", "pool", "sp", "dma")
ENG_OF = {"pe": "tensor", "act": "scalar", "dve": "vector", "pool": "gpsimd",
          "sp": "sync", "dma": "sync"}


class Sch:

    def __init__(self):
        self.ops = []
        self.res = {}
        self.nbar = 0
        self.last_real = {}

    def op(self, ch, fn, r=(), w=(), pr=(), tag=""):
        idx = len(self.ops)
        if tag != "bar":
            self.last_real[ch] = idx
        deps = {}

        def dep(j, ptr):
            if j is not None:
                deps[j] = deps.get(j, False) or ptr

        for name in r:
            e = self.res.get(name)
            if e:
                dep(e["w"], False)
                if name.startswith("ps"):
                    for j in e["r"]:
                        dep(j, False)
        for name in pr:
            e = self.res.get(name)
            if e:
                dep(e["w"], True)
        for name in w:
            e = self.res.get(name)
            if e:
                dep(e["w"], False)
                for j in e["r"]:
                    dep(j, False)
        deps.pop(idx, None)
        self.ops.append({"ch": ch, "fn": fn, "deps": deps,
                         "inc": ch == "dma", "tag": tag})
        for name in list(r) + list(pr):
            self.res.setdefault(name, {"w": None, "r": []})["r"].append(idx)
        for name in w:
            self.res[name] = {"w": idx, "r": []}
        return idx

    def dma(self, fn, r=(), w=(), tag=""):
        return self.op("dma", fn, r=r, w=list(w) + ["_dmalast"], tag=tag)

    def barrier(self):
        snap = dict(self.last_real)
        for ch in ("pe", "act", "dve", "pool", "sp"):
            deps = {j: False for c2, j in snap.items() if c2 != ch}
            self.ops.append({"ch": ch, "fn": lambda e: e.nop(),
                             "deps": deps, "inc": False, "tag": "bar"})

    def plan(self):
        for o in self.ops:
            need = {}
            for j, is_ptr in o["deps"].items():
                p = self.ops[j]
                if p["ch"] == o["ch"] and p["ch"] != "dma" and not is_ptr:
                    continue
                p["inc"] = True
                need[j] = True
            o["need"] = sorted(need)
        cnt = {c: 0 for c in CHANNELS}
        for o in self.ops:
            if o["inc"]:
                cnt[o["ch"]] += 16 if o["ch"] == "dma" else 1
                o["val"] = cnt[o["ch"]]

    def emit(self, nc, sems):
        hw = {c: {} for c in CHANNELS}
        for o in self.ops:
            eng = getattr(nc, ENG_OF[o["ch"]])
            ech = "sp" if o["ch"] == "dma" else o["ch"]
            want = {}
            for j in o["need"]:
                p = self.ops[j]
                want[p["ch"]] = max(want.get(p["ch"], 0), p["val"])
            for pch, val in want.items():
                if hw[ech].get(pch, 0) >= val:
                    continue
                eng.wait_ge(sems[pch], val)
                hw[ech][pch] = val
            inst = o["fn"](eng)
            if o["inc"]:
                inst.then_inc(sems[o["ch"]], 16 if o["ch"] == "dma" else 1)


class Arena:

    def __init__(self, handle, nbytes):
        self.h = handle
        self.n = nbytes
        self.off = 0
        self.marks = []

    def push(self):
        self.marks.append(self.off)

    def pop(self):
        self.off = self.marks.pop()

    def take(self, nbytes, dt, parts=128):
        off = (self.off + 63) & ~63
        self.off = off + nbytes
        assert self.off <= self.n, "arena overflow: %d > %d" % (self.off, self.n)
        ap = self.h[0:parts, off:off + nbytes]
        return ap.bitcast(dt) if dt != U8 else ap

    def f32(self, cols, parts=128):
        return self.take(cols * 4, F32, parts)

    def bf16(self, cols, parts=128):
        return self.take(cols * 2, BF16, parts)


PSF = [["psF0a", "psF0b"], ["psF1a", "psF1b"]]


def build_body(nc, sch, v, dram, P, psum):
    ar = P["arena"]

    sch.op("dve", lambda e: e.memset(P["negC"][:], -CSH), w=["negC"])
    sch.op("dve", lambda e: e.memset(P["zero1"][:], 0.0), w=["zero1"])
    sch.op("dve", lambda e: e.memset(P["eps"][:], EPS), w=["eps"])
    sch.op("dve", lambda e: e.memset(P["ones64"][:], 1.0), w=["ones64"])
    sch.dma(lambda e: e.dma_start(out=P["identf"][:], in_=dram["ident"]),
            w=["identf"])
    sch.dma(lambda e: e.dma_start(out=P["trif"][:], in_=dram["tri"]), w=["trif"])
    sch.op("pool", lambda e: e.tensor_copy(P["identb"][:], P["identf"][:]),
           r=["identf"], w=["identb"])
    sch.op("pool", lambda e: e.tensor_copy(P["trib"][:], P["trif"][:]),
           r=["trif"], w=["trib"])
    for c in range(8):
        sch.dma(lambda e, c=c: e.dma_start(
            out=P["gwT"][:, 8 * c:8 * c + 8],
            in_=dram["gwT"][128 * c:128 * c + 128, :]), w=["gwT%d" % c])

    ar.push()
    xn_own = ar.bf16(8 * 2048)
    ar.push()
    xn_oth = ar.bf16(8 * 2048)
    ar.push()
    xst = [ar.f32(D) for _ in range(2)]
    xnt = [ar.f32(D) for _ in range(2)]
    xnb = [ar.bf16(D) for _ in range(2)]
    xT = [ar.f32(128) for _ in range(4)]
    g_all = ar.f32(16 * 8)
    stat2 = [ar.f32(8) for _ in range(2)]
    mall = ar.f32(16)
    gem = ar.f32(8 * 16)
    tkk = ar.f32(16)
    uu1 = ar.f32(16)

    def xnf(c, tb):
        return "xnf.%d.%d" % (c, tb // 4)

    def p1_front(tb):
        r = tb % 2
        stat = stat2[r]
        sch.dma(lambda e, tb=tb, r=r: e.dma_start(
            out=xst[r], in_=dram["xseq"][128 * tb:128 * tb + 128, :]),
            w=["xst%d" % r], tag="ldx")
        sch.op("dve", lambda e, r=r: e.reduce_sum(
            stat2[r][:, 0:1], xst[r], axis=AX.X),
            r=["xst%d" % r], w=["stat_s%d" % r])
        sch.op("act", lambda e, r=r: e.activation(
            xnt[r], xst[r], AF.Square, accum_out=stat2[r][:, 1:2]),
            r=["xst%d" % r], w=["stat_q%d" % r, "xnt%d" % r])
        sch.op("dve", lambda e, stat=stat, r=r: e.tensor_scalar(
            stat[:, 2:3], stat[:, 0:1], 1.0 / D, None, ALU.mult),
            pr=["stat_s%d" % r], w=["stat_m%d" % r])
        sch.op("dve", lambda e, stat=stat, r=r: e.tensor_mul(
            stat[:, 3:4], stat[:, 2:3], stat[:, 2:3]),
            pr=["stat_m%d" % r], w=["stat_m2%d" % r])
        sch.op("dve", lambda e, stat=stat, r=r: e.tensor_scalar(
            stat[:, 4:5], stat[:, 1:2], 1.0 / D, None, ALU.mult),
            pr=["stat_q%d" % r], w=["stat_v%d" % r])
        sch.op("dve", lambda e, stat=stat, r=r: e.tensor_sub(
            stat[:, 4:5], stat[:, 4:5], stat[:, 3:4]),
            pr=["stat_m2%d" % r, "stat_v%d" % r], w=["stat_v%d" % r])
        sch.op("act", lambda e, stat=stat, r=r: e.activation(
            stat[:, 5:6], stat[:, 4:5], AF.Ln, bias=P["eps"][:, 0:1], scale=1.0),
            pr=["eps", "stat_v%d" % r], w=["stat_lv%d" % r])
        sch.op("act", lambda e, stat=stat, r=r: e.activation(
            stat[:, 6:7], stat[:, 5:6], AF.Exp, bias=P["zero1"][:, 0:1],
            scale=-0.5),
            pr=["zero1", "stat_lv%d" % r], w=["stat_r%d" % r])

    def p1_back(tb):
        r = tb % 2
        own = tb < 16
        stat = stat2[r]
        sch.op("dve", lambda e, r=r, stat=stat: e.tensor_scalar(
            xnt[r], xst[r], stat[:, 2:3], stat[:, 6:7], ALU.subtract, ALU.mult),
            r=["xst%d" % r], pr=["stat_m%d" % r, "stat_r%d" % r],
            w=["xnt%d" % r])
        if own:
            sch.op("dve", lambda e, tb=tb, r=r: e.tensor_add(
                P["s"][:, D * tb:D * tb + D], xst[r], xnt[r]),
                r=["xst%d" % r, "xnt%d" % r], w=["s.%d" % tb])
        sch.op("dve", lambda e, r=r: e.tensor_copy(xnb[r], xnt[r]),
               r=["xnt%d" % r], w=["xnb%d" % r])
        for c in range(8):
            p2 = c % 2
            pst = psum["trb"][p2]
            pn = PSF[0][p2]
            if own:
                dst, col = xn_own, 2048 * c + 128 * tb
            else:
                dst, col = xn_oth, 2048 * c + 128 * (tb - 16)
            sch.op("pe", lambda e, c=c, r=r, p=pst: e.transpose(
                p, xnb[r][:, 128 * c:128 * c + 128], P["identb"][:]),
                r=["xnb%d" % r, "identb"], w=[pn], tag="trb")
            sch.op("act", lambda e, p=pst, d=dst, col=col: e.copy(
                d[:, col:col + 128], p),
                r=[pn], w=[xnf(c, tb)])
        if own:
            for half in range(2):
                for ci in range(4):
                    c = 4 * half + ci
                    p2 = c % 2
                    pst = psum["trf"][p2]
                    pn = PSF[1][p2]
                    sch.op("pe", lambda e, r=r, c=c, p=pst: e.transpose(
                        p, xnt[r][:, 128 * c:128 * c + 128], P["identf"][:]),
                        r=["xnt%d" % r, "identf"], w=[pn], tag="trf")
                    sch.op("act", lambda e, p=pst, ci=ci: e.copy(xT[ci], p),
                           r=[pn], w=["xT%d" % ci])

                def gfn(e, half=half):
                    for ci in range(4):
                        c = 4 * half + ci
                        i = e.matmul(psum["g8"], xT[ci],
                                     P["gwT"][:, 8 * c:8 * c + 8],
                                     start=(c == 0), stop=(c == 7))
                    return i
                sch.op("pe", gfn,
                       r=["xT%d" % ci for ci in range(4)] +
                         ["gwT%d" % c for c in range(8)] + ["psC"],
                       w=["psC"], tag="gate")
            sch.op("dve", lambda e, tb=tb: e.tensor_copy(
                g_all[:, 8 * tb:8 * tb + 8], psum["g8"]),
                r=["psC"], w=["gall%d" % tb])

    for tb in range(NTB):
        p1_front(tb)
        if tb > 0:
            p1_back(tb - 1)
    p1_back(NTB - 1)

    gv = g_all.rearrange("p (t e) -> p t e", t=16)
    gemv = gem.rearrange("p (e t) -> p e t", e=8)
    mgv = P["masksg"][:].rearrange("p (e t) -> p e t", e=8)
    gall = ["gall%d" % t for t in range(16)]
    sch.op("dve", lambda e: e.reduce_max(mall[:, :], gv, axis=AX.X),
           r=gall, w=["mall"])
    sch.op("dve", lambda e: e.tensor_copy(gemv, gv.transpose([0, 2, 1])),
           r=gall, pr=["mall"], w=["gem"])
    mbc = bass.AP(tensor=mall.tensor, offset=mall.offset,
                  ap=[mall.ap[0], [0, 8], [1, 16]])
    sch.op("dve", lambda e: e.tensor_tensor(mgv, gemv, mbc, ALU.is_equal),
           pr=["gem", "mall"], w=["raw"])
    sch.op("dve", lambda e: e.memset(tkk, 0.0), w=["tk"])
    for ei in range(8):
        sch.op("dve", lambda e: e.tensor_scalar(
            uu1, tkk, -1.0, 1.0, ALU.mult, ALU.add), pr=["tk"], w=["uu"])
        sch.op("dve", lambda e, ei=ei: e.tensor_mul(
            mgv[:, ei, :], mgv[:, ei, :], uu1),
            pr=["raw", "uu"], w=["mask%d" % ei, "raw"])
        sch.op("dve", lambda e, ei=ei: e.tensor_add(tkk, tkk, mgv[:, ei, :]),
               pr=["mask%d" % ei, "tk"], w=["tk"])
    sch.barrier()
    ar.pop()

    ar.push()
    wkst = ar.f32(8 * 128)
    wkb = ar.bf16(8 * 128)
    for c in range(8):
        sch.dma(lambda e, c=c: e.dma_start(
            out=wkst[:, 128 * c:128 * c + 128],
            in_=dram["wkvd"][128 * c:128 * c + 128, :]), w=["wkst%d" % c])
        sch.op("pool", lambda e, c=c: e.tensor_copy(
            wkb[:, 128 * c:128 * c + 128], wkst[:, 128 * c:128 * c + 128]),
            r=["wkst%d" % c], w=["wkb%d" % c])
    for b5 in range(8):
        pp = psum["half"][b5 % 2]
        ppn = "psC" if b5 % 2 == 0 else "psD"
        xn_src = xn_own if b5 < 4 else xn_oth
        bl = b5 % 4

        def fn(e, pp=pp, xn_src=xn_src, bl=bl):
            for c in range(8):
                i = e.matmul(pp, wkb[:, 128 * c:128 * c + 128],
                             xn_src[:, 2048 * c + 512 * bl:2048 * c + 512 * bl + 512],
                             start=(c == 0), stop=(c == 7))
            return i
        sch.op("pe", fn,
               r=["wkb%d" % c for c in range(8)] +
                 ["xnf.%d.%d" % (c, b5) for c in range(8)] + [ppn],
               w=[ppn], tag="ckv")
        sch.op("act", lambda e, pp=pp, b5=b5: e.copy(
            P["ckv"][:, 512 * b5:512 * b5 + 512], pp),
            r=[ppn], w=["ckv%d" % b5])
    sch.barrier()
    ar.pop()
    ar.pop()

    ar.push()
    est = [ar.f32(D) for _ in range(4)]
    ebf = [[ar.bf16(D) for _ in range(8)] for _ in range(2)]
    for ei in range(8):
        er = ei % 2
        for hc in range(8):
            sr = (ei * 8 + hc) % 4
            sch.dma(lambda e, ei=ei, hc=hc, sr=sr: e.dma_start(
                out=est[sr], in_=dram["weT"][ei, 128 * hc:128 * hc + 128, :]),
                w=["est%d" % sr], tag="ldw_e")
            sch.op("act", lambda e, er=er, hc=hc, sr=sr: e.copy(
                ebf[er][hc], est[sr]),
                r=["est%d" % sr], w=["ebf%d_%d" % (er, hc)])
        for tb in range(16):
            pp = psum["full"][tb % 2]
            pns = PSF[tb % 2]

            def fn(e, er=er, tb=tb, pp=pp):
                for hc in range(8):
                    lhs = xn_own[:, 2048 * hc + 128 * tb:2048 * hc + 128 * tb + 128]
                    e.matmul(pp[:, 0:512], lhs, ebf[er][hc][:, 0:512],
                             start=(hc == 0), stop=(hc == 7))
                    i = e.matmul(pp[:, 512:1024], lhs, ebf[er][hc][:, 512:1024],
                                 start=(hc == 0), stop=(hc == 7))
                return i
            sch.op("pe", fn,
                   r=["ebf%d_%d" % (er, hc) for hc in range(8)] +
                     ["xnf.%d.%d" % (c, tb // 4) for c in range(8)] + pns,
                   w=pns, tag="ffn")
            for hf in range(2):
                sch.op("dve", lambda e, tb=tb, ei=ei, hf=hf, pp=pp:
                       e.scalar_tensor_tensor(
                           P["s"][:, D * tb + 512 * hf:D * tb + 512 * hf + 512],
                           pp[:, 512 * hf:512 * hf + 512],
                           mgv[:, ei, tb:tb + 1],
                           P["s"][:, D * tb + 512 * hf:D * tb + 512 * hf + 512],
                           ALU.mult, ALU.add),
                       r=[pns[hf], "s.%d" % tb], pr=["mask%d" % ei],
                       w=["s.%d" % tb], tag="ffn_ev")
    sch.barrier()
    ar.pop()

    ar.push()
    wqst = [ar.f32(D) for _ in range(2)]
    wqb = [ar.bf16(D) for _ in range(8)]
    wrst = ar.f32(8 * 32)
    wrot = ar.f32(8 * 32)
    wkst2 = ar.f32(32)
    wkrot = ar.f32(32)
    wqrb = ar.bf16(8 * 32)
    wqrrb = ar.bf16(8 * 32)
    wkrb = ar.bf16(64)
    cst = [ar.f32(512, parts=32) for _ in range(2)]
    sst = [ar.f32(512, parts=32) for _ in range(2)]
    t1 = ar.f32(512, parts=32)
    t2 = ar.f32(512, parts=32)

    for c in range(8):
        r = c % 2
        sch.dma(lambda e, c=c, r=r: e.dma_start(
            out=wqst[r], in_=dram["wq"][128 * c:128 * c + 128, :]),
            w=["wqst%d" % r], tag="ldwq")
        sch.op("act", lambda e, c=c, r=r: e.mul(wqb[c], wqst[r], SCALE),
            r=["wqst%d" % r], w=["wqb%d" % c])
    for do in range(8):
        for bl in range(4):
            pp = psum["half"][bl % 2]
            ppn = "psC" if bl % 2 == 0 else "psD"

            def fn(e, do=do, bl=bl, pp=pp):
                for c in range(8):
                    i = e.matmul(
                        pp, wqb[c][:, 128 * do:128 * do + 128],
                        xn_own[:, 2048 * c + 512 * bl:2048 * c + 512 * bl + 512],
                        start=(c == 0), stop=(c == 7))
                return i
            sch.op("pe", fn,
                   r=["wqb%d" % c for c in range(8)] +
                     ["xnf.%d.%d" % (c, bl) for c in range(8)] + [ppn],
                   w=[ppn], tag="q")
            sch.op("act", lambda e, pp=pp, do=do, bl=bl: e.copy(
                P["q"][:, 2048 * do + 512 * bl:2048 * do + 512 * bl + 512], pp),
                r=[ppn], w=["q.%d.%d" % (do, bl)])

    sch.dma(lambda e: e.dma_start(out=wkst2[:, 0:32], in_=dram["wkr"]),
            w=["wkrst"])
    sch.op("dve", lambda e: e.tensor_scalar(
        wkrot[:, 0:16], wkst2[:, 16:32], -1.0, None, ALU.mult),
        r=["wkrst"], w=["wkro"])
    sch.op("dve", lambda e: e.tensor_copy(wkrot[:, 16:32], wkst2[:, 0:16]),
           r=["wkrst"], w=["wkro2"])
    sch.op("pool", lambda e: e.tensor_copy(wkrb[:, 0:32], wkst2[:, 0:32]),
           r=["wkrst"], w=["wkrb"])
    sch.op("pool", lambda e: e.tensor_copy(wkrb[:, 32:64], wkrot[:, 0:32]),
           r=["wkro", "wkro2"], w=["wkrb2"])
    for c in range(8):
        sch.dma(lambda e, c=c: e.dma_start(
            out=wrst[:, 32 * c:32 * c + 32],
            in_=dram["wqr"][128 * c:128 * c + 128, :]), w=["wqrst%d" % c])
        sch.op("dve", lambda e, c=c: e.tensor_scalar(
            wrot[:, 32 * c:32 * c + 16], wrst[:, 32 * c + 16:32 * c + 32],
            -1.0, None, ALU.mult), r=["wqrst%d" % c], w=["wqro%d" % c])
        sch.op("dve", lambda e, c=c: e.tensor_copy(
            wrot[:, 32 * c + 16:32 * c + 32], wrst[:, 32 * c:32 * c + 16]),
            r=["wqrst%d" % c], w=["wqro2%d" % c])
        sch.op("dve", lambda e, c=c: e.tensor_scalar(
            wqrb[:, 32 * c:32 * c + 32], wrst[:, 32 * c:32 * c + 32],
            SCALE, None, ALU.mult), r=["wqrst%d" % c], w=["wqrb%d" % c])
        sch.op("dve", lambda e, c=c: e.tensor_scalar(
            wqrrb[:, 32 * c:32 * c + 32], wrot[:, 32 * c:32 * c + 32],
            SCALE, None, ALU.mult),
            r=["wqro%d" % c, "wqro2%d" % c], w=["wqrrb%d" % c])

    for b5 in range(8):
        ppa, ppb = psum["half"][0], psum["half"][1]
        cr = b5 % 2
        sch.dma(lambda e, b5=b5, cr=cr: e.dma_start(
            out=cst[cr], in_=dram["cosF"][:, 512 * b5:512 * b5 + 512]),
            w=["cst%d" % cr])
        sch.dma(lambda e, b5=b5, cr=cr: e.dma_start(
            out=sst[cr], in_=dram["sinF"][:, 512 * b5:512 * b5 + 512]),
            w=["sst%d" % cr])
        sch.op("pe", lambda e, b5=b5, p=ppa: e.matmul(
            p[0:32, :], wkrb[:, 0:32], P["ckv"][:, 512 * b5:512 * b5 + 512],
            start=True, stop=True),
            r=["wkrb", "ckv%d" % b5, "psC"], w=["psC"], tag="kr")
        sch.op("pe", lambda e, b5=b5, p=ppb: e.matmul(
            p[0:32, :], wkrb[:, 32:64], P["ckv"][:, 512 * b5:512 * b5 + 512],
            start=True, stop=True),
            r=["wkrb2", "ckv%d" % b5, "psD"], w=["psD"], tag="krr")
        sch.op("dve", lambda e, cr=cr, p=ppa: e.tensor_tensor(
            t1[0:32, :], p[0:32, :], cst[cr], ALU.mult),
            r=["psC", "cst%d" % cr], w=["t1"])
        sch.op("dve", lambda e, cr=cr, p=ppb: e.tensor_tensor(
            t2[0:32, :], p[0:32, :], sst[cr], ALU.mult),
            r=["psD", "sst%d" % cr], w=["t2"])
        sch.op("dve", lambda e, b5=b5: e.tensor_add(
            P["kr"][0:32, 512 * b5:512 * b5 + 512], t1[0:32, :], t2[0:32, :]),
            r=["t1", "t2"], w=["kr.%d" % b5])
        sch.op("dve", lambda e, b5=b5: e.tensor_add(
            P["kr"][64:96, 512 * b5:512 * b5 + 512], t1[0:32, :], t2[0:32, :]),
            r=["t1", "t2"], w=["krd.%d" % b5])

    for bl in range(4):
        ppa, ppb = psum["half"][0], psum["half"][1]
        cr = bl % 2

        def fq(e, bl=bl, p=ppa):
            for c in range(8):
                i = e.matmul(
                    p[0:32, :], wqrb[:, 32 * c:32 * c + 32],
                    xn_own[:, 2048 * c + 512 * bl:2048 * c + 512 * bl + 512],
                    start=(c == 0), stop=(c == 7))
            return i

        def fqr(e, bl=bl, p=ppb):
            for c in range(8):
                i = e.matmul(
                    p[0:32, :], wqrrb[:, 32 * c:32 * c + 32],
                    xn_own[:, 2048 * c + 512 * bl:2048 * c + 512 * bl + 512],
                    start=(c == 0), stop=(c == 7))
            return i
        sch.op("pe", fq, r=["wqrb%d" % c for c in range(8)] +
               ["xnf.%d.%d" % (c, bl) for c in range(8)] + ["psC"],
               w=["psC"], tag="qr")
        sch.op("pe", fqr, r=["wqrrb%d" % c for c in range(8)] +
               ["xnf.%d.%d" % (c, bl) for c in range(8)] + ["psD"],
               w=["psD"], tag="qrr")
        sch.dma(lambda e, bl=bl, cr=cr: e.dma_start(
            out=cst[cr], in_=dram["cosF"][:, 512 * bl:512 * bl + 512]),
            w=["cst%d" % cr])
        sch.dma(lambda e, bl=bl, cr=cr: e.dma_start(
            out=sst[cr], in_=dram["sinF"][:, 512 * bl:512 * bl + 512]),
            w=["sst%d" % cr])
        sch.op("dve", lambda e, cr=cr, p=ppa: e.tensor_tensor(
            t1[0:32, :], p[0:32, :], cst[cr], ALU.mult),
            r=["psC", "cst%d" % cr], w=["t1"])
        sch.op("dve", lambda e, cr=cr, p=ppb: e.tensor_tensor(
            t2[0:32, :], p[0:32, :], sst[cr], ALU.mult),
            r=["psD", "sst%d" % cr], w=["t2"])
        sch.op("dve", lambda e, bl=bl: e.tensor_add(
            P["qr"][0:32, 512 * bl:512 * bl + 512], t1[0:32, :], t2[0:32, :]),
            r=["t1", "t2"], w=["qr.%d" % bl])
        sch.op("dve", lambda e, bl=bl: e.tensor_add(
            P["qr"][64:96, 512 * bl:512 * bl + 512], t1[0:32, :], t2[0:32, :]),
            r=["t1", "t2"], w=["qrd.%d" % bl])
    sch.barrier()
    ar.pop()
    ar.pop()

    ar.push()
    wost = [ar.f32(D) for _ in range(1)]
    wob = [ar.bf16(D) for _ in range(8)]
    wust = ar.f32(D)
    wub = ar.bf16(D)
    wvst = ar.f32(D)
    wvb = ar.bf16(D)
    kq = [ar.bf16(T) for _ in range(2)]
    vq = ar.bf16(32 * 384)
    pvA = ar.f32(512)
    pvB = ar.f32(512)
    onesf = ar.f32(64)
    pt = [ar.bf16(1024) for _ in range(2)]
    afm = [[ar.bf16(512) for _ in range(2)] for _ in range(2)]
    rcp = [ar.f32(512) for _ in range(1)]

    for c in range(8):
        sch.dma(lambda e, c=c: e.dma_start(
            out=wost[0], in_=dram["wo"][128 * c:128 * c + 128, :]),
            w=["wost0"], tag="ldwo")
        sch.op("dve", lambda e, c=c: e.tensor_copy(wob[c], wost[0]),
               r=["wost0"], w=["wob%d" % c])
    def vqones_fn(e):
        for tb in range(NTB):
            e.memset(vq[:, 384 * tb + 64:384 * tb + 128], 1.0)
            i = e.memset(vq[:, 384 * tb + 256:384 * tb + 320], 1.0)
        return i
    sch.op("dve", vqones_fn, w=["vqones"])
    sch.op("dve", lambda e: e.memset(onesf, 1.0), w=["onesf"])
    sch.dma(lambda e: e.dma_start(out=wust, in_=dram["wku"]), w=["wust"])
    sch.op("dve", lambda e: e.tensor_copy(wub, wust), r=["wust"], w=["wub"])
    sch.dma(lambda e: e.dma_start(out=wvst, in_=dram["wvu"]), w=["wvst"])
    sch.op("dve", lambda e: e.tensor_copy(wvb, wvst), r=["wvst"], w=["wvb"])

    slot = SLOT[v]
    perm = PERM[v]

    for pp4 in range(4):
        for dl in range(2):
            do = 2 * pp4 + dl
            for b5 in range(8):
                pb = psum["half"][b5 % 2]
                ppn = "psC" if b5 % 2 == 0 else "psD"
                sch.op("pe", lambda e, do=do, b5=b5, pb=pb: e.matmul(
                    pb, wub[:, 128 * do:128 * do + 128],
                    P["ckv"][:, 512 * b5:512 * b5 + 512],
                    start=True, stop=True),
                    r=["wub", "ckv%d" % b5, ppn], w=[ppn], tag="kup")
                sch.op("dve", lambda e, dl=dl, b5=b5, pb=pb: e.tensor_copy(
                    kq[dl][:, 512 * b5:512 * b5 + 512], pb),
                    r=[ppn], w=["kq.%d.%d" % (dl, b5)])
        for tb in range(NTB):
            pb = psum["half"][tb % 2]
            ppn = "psC" if tb % 2 == 0 else "psD"
            sch.op("pe", lambda e, tb=tb, pp4=pp4, pb=pb: e.matmul(
                pb[:, 0:256], P["ckv"][:, 128 * tb:128 * tb + 128],
                wvb[:, 256 * pp4:256 * pp4 + 256], start=True, stop=True),
                r=["wvb", "ckv%d" % (tb // 4), ppn], w=[ppn], tag="vup")
            def vev(e, tb=tb, pb=pb):
                for j, off in enumerate((0, 128, 192, 320)):
                    i = e.tensor_copy(
                        vq[:, 384 * tb + off:384 * tb + off + 64],
                        pb[:, 64 * j:64 * j + 64])
                return i
            sch.op("dve", vev, r=[ppn, "vqones"], w=["vq.%d" % tb])

        for g in range(4):
            gt = 1024 * perm[g // 2] + 512 * (g % 2)
            nrect = gt // 128
            kcs = [(8 * slot[kc // 8] + kc % 8, None) for kc in range(nrect)]
            kcs += [(4 * g + j, j) for j in range(4)]
            n_kc = len(kcs)
            for hl in range(2):
                hp = 2 * pp4 + hl
                dl = hl
                pend = []
                for ik, (sc, dj) in enumerate(kcs):
                    ps = psum["full"][ik % 2]
                    pns = PSF[ik % 2]
                    ptb = pt[ik % 2]
                    ptn = "pt%d" % (ik % 2)
                    q0 = 0 if dj is None else 128 * dj
                    first = ik == 0
                    last = ik == n_kc - 1

                    def smm(e, sc=sc, dl=dl, hp=hp, g=g, ps=ps, q0=q0):
                        kcol = 128 * sc
                        qc = 2048 * hp + 512 * g + q0
                        e.matmul(ps[:, q0:512],
                                 kq[dl][0:64, kcol:kcol + 128],
                                 P["q"][0:64, qc:qc + 512 - q0],
                                 start=True, stop=False,
                                 tile_position=(0, 0))
                        e.matmul(ps[:, 512 + q0:1024],
                                 kq[dl][64:128, kcol:kcol + 128],
                                 P["q"][64:128, qc:qc + 512 - q0],
                                 start=True, stop=False,
                                 tile_position=(64, 0))
                        e.matmul(ps[:, q0:512],
                                 P["kr"][0:32, kcol:kcol + 128],
                                 P["qr"][0:32, 512 * g + q0:512 * g + 512],
                                 start=False, stop=True,
                                 tile_position=(0, 0))
                        i = e.matmul(ps[:, 512 + q0:1024],
                                     P["kr"][64:96, kcol:kcol + 128],
                                     P["qr"][64:96, 512 * g + q0:512 * g + 512],
                                     start=False, stop=True,
                                     tile_position=(64, 0))
                        return i
                    sch.op("pe", smm,
                           r=["kq.%d.%d" % (dl, sc // 4), "q.%d.%d" % (hp, g),
                              "kr.%d" % (sc // 4), "krd.%d" % (sc // 4),
                              "qr.%d" % g, "qrd.%d" % g] + pns,
                           w=pns, tag="S")

                    if dj is None:
                        sch.op("act", lambda e, ps=ps, ptb=ptb: e.activation(
                            ptb, ps, AF.Exp, bias=P["negC"][:, 0:1], scale=1.0),
                            r=pns, pr=["negC"], w=[ptn], tag="exp")
                    else:
                        def efn(e, ps=ps, ptb=ptb, q0=q0):
                            e.activation(ptb[:, q0:512], ps[:, q0:512], AF.Exp,
                                         bias=P["negC"][:, 0:1], scale=1.0)
                            return e.activation(
                                ptb[:, 512 + q0:1024], ps[:, 512 + q0:1024],
                                AF.Exp, bias=P["negC"][:, 0:1], scale=1.0)
                        sch.op("act", efn, r=pns, pr=["negC"], w=[ptn],
                               tag="expd")

                        def mfn(e, ptb=ptb, q0=q0):
                            e.tensor_mul(ptb[:, q0:q0 + 128],
                                         ptb[:, q0:q0 + 128], P["trib"][:])
                            return e.tensor_mul(
                                ptb[:, 512 + q0:512 + q0 + 128],
                                ptb[:, 512 + q0:512 + q0 + 128], P["trib"][:])
                        sch.op("dve", mfn, r=[ptn, "trib"], w=[ptn],
                               tag="trimask")

                    def pvfn(e, sc=sc, hl=hl, ptb=ptb, q0=q0,
                             first=first, last=last):
                        va = 384 * sc + 192 * hl
                        e.matmul(psum["pv"][:, q0:512],
                                 vq[:, va:va + 128], ptb[:, q0:512],
                                 start=first, stop=last)
                        i = e.matmul(psum["sm"][:, q0:512],
                                     vq[:, va + 64:va + 192],
                                     ptb[:, 512 + q0:1024],
                                     start=first, stop=last)
                        return i
                    if pend:
                        pf, pr_ = pend.pop()
                        sch.op("pe", pf, r=pr_, w=["psC", "psD"], tag="pv")
                    pend.append((pvfn, ["vq.%d" % sc, ptn, "ones64",
                                        "psC", "psD"]))
                if pend:
                    pf, pr_ = pend.pop()
                    sch.op("pe", pf, r=pr_, w=["psC", "psD"], tag="pv")

                sch.op("dve", lambda e: e.tensor_copy(pvA, psum["pv"][:, :]),
                       r=["psC"], w=["pvA"], tag="pvev")
                sch.op("dve", lambda e: e.tensor_copy(pvB, psum["sm"][:, :]),
                       r=["psD"], w=["pvB"], tag="pvev")

                def bfn(e):
                    e.matmul(psum["bc"][0:64, 0:512],
                             onesf[64:65, 0:64], pvA[64:65, :],
                             start=True, stop=True, tile_position=(64, 0))
                    return e.matmul(psum["bc"][64:128, 0:512],
                                    onesf[0:1, 0:64], pvB[0:1, :],
                                    start=True, stop=True,
                                    tile_position=(0, 64))
                sch.op("pe", bfn, r=["pvA", "pvB", "onesf", "psE"],
                       w=["psE"], tag="bcast")
                sch.op("act", lambda e: e.activation(
                    rcp[0], psum["bc"][:, :], AF.Ln,
                    bias=P["zero1"][:, 0:1], scale=1.0),
                    r=["psE"], pr=["zero1"], w=["rcp0"])
                sch.op("act", lambda e: e.activation(
                    rcp[0], rcp[0], AF.Exp, bias=P["zero1"][:, 0:1],
                    scale=-1.0),
                    pr=["zero1", "rcp0"], w=["rcp0"])

                aring = g % 2

                def nfn(e, aring=aring, hl=hl):
                    e.tensor_tensor(afm[aring][hl][0:64, :],
                                    pvA[0:64, :], rcp[0][0:64, :], ALU.mult)
                    return e.tensor_tensor(afm[aring][hl][64:128, :],
                                           pvB[64:128, :],
                                           rcp[0][64:128, :], ALU.mult)
                sch.op("dve", nfn,
                    r=["pvA", "pvB"], pr=["rcp0"],
                    w=["afm.%d.%d" % (aring, hl)], tag="norm")

            ring = g % 2
            for qt in range(4):
                pw = psum["full"][qt % 2]
                pns = PSF[qt % 2]

                def wofn(e, pp4=pp4, ring=ring, qt=qt, pw=pw):
                    for dl in range(2):
                        a = afm[ring][dl][:, 128 * qt:128 * qt + 128]
                        wo_c = wob[2 * pp4 + dl]
                        e.matmul(pw[:, 0:512], a, wo_c[:, 0:512],
                                 start=(dl == 0), stop=(dl == 1))
                        i = e.matmul(pw[:, 512:1024], a, wo_c[:, 512:1024],
                                     start=(dl == 0), stop=(dl == 1))
                    return i
                sch.op("pe", wofn,
                       r=["afm.%d.0" % ring, "afm.%d.1" % ring,
                          "wob%d" % (2 * pp4), "wob%d" % (2 * pp4 + 1)] + pns,
                       w=pns, tag="wo")
                tbq = 4 * g + qt
                sch.op("dve", lambda e, tbq=tbq, pw=pw: e.tensor_add(
                    P["s"][:, D * tbq:D * tbq + D],
                    P["s"][:, D * tbq:D * tbq + D], pw),
                    r=pns + ["s.%d" % tbq], w=["s.%d" % tbq], tag="wo_ev")
    sch.barrier()
    ar.pop()

    outv = dram["out"].rearrange("(t p) d -> p t d", p=128)
    sv = P["s"][:, :].rearrange("p (t d) -> p t d", t=16)
    sch.dma(lambda e: e.dma_start(out=outv, in_=sv),
            r=["s.%d" % t for t in range(16)], w=["outdma"], tag="stout")
    sch.op("sp", lambda e: e.nop(), r=["_dmalast"], tag="final")


def build_nc():
    nc = bass.Bass("TRN2", target_bir_lowering=False)
    dram = {
        "xseq": nc.dram_tensor("xseq", [T, D], F32, kind="ExternalInput").ap(),
        "wkvd": nc.dram_tensor("wkvd", [D, DC], F32, kind="ExternalInput").ap(),
        "wku": nc.dram_tensor("wku", [DC, D], F32, kind="ExternalInput").ap(),
        "wvu": nc.dram_tensor("wvu", [DC, D], F32, kind="ExternalInput").ap(),
        "wq": nc.dram_tensor("wq", [D, D], F32, kind="ExternalInput").ap(),
        "wqr": nc.dram_tensor("wqr", [D, DR], F32, kind="ExternalInput").ap(),
        "wkr": nc.dram_tensor("wkr", [DC, DR], F32, kind="ExternalInput").ap(),
        "wo": nc.dram_tensor("wo", [D, D], F32, kind="ExternalInput").ap(),
        "gwT": nc.dram_tensor("gwT", [D, 8], F32, kind="ExternalInput").ap(),
        "weT": nc.dram_tensor("weT", [8, D, D], F32, kind="ExternalInput").ap(),
        "cosF": nc.dram_tensor("cosF", [DR, T], F32, kind="ExternalInput").ap(),
        "sinF": nc.dram_tensor("sinF", [DR, T], F32, kind="ExternalInput").ap(),
        "ident": nc.dram_tensor("ident", [128, 128], F32,
                                kind="ExternalInput").ap(),
        "tri": nc.dram_tensor("tri", [128, 128], F32, kind="ExternalInput").ap(),
        "out": nc.dram_tensor("out", [2048, D], F32, kind="ExternalOutput").ap(),
    }
    with contextlib.ExitStack() as ctx:
        sb = lambda n, s, d: ctx.enter_context(nc.sbuf_tensor(n, s, d))
        P = {}
        P["s"] = sb("s_res", [128, 16 * D], F32)
        P["ckv"] = sb("ckv", [128, T], BF16)
        P["q"] = sb("q_fm", [128, 8 * 2048], BF16)
        P["kr"] = sb("kr_d", [128, T], BF16)
        P["qr"] = sb("qr_d", [128, 2048], BF16)
        P["masksg"] = sb("masksg", [128, 8 * 16], F32)
        P["gwT"] = sb("gwT_sb", [128, 64], F32)
        P["negC"] = sb("negC", [128, 1], F32)
        P["zero1"] = sb("zero1", [128, 1], F32)
        P["eps"] = sb("eps_c", [128, 1], F32)
        P["ones64"] = sb("ones64", [128, 64], BF16)
        P["identf"] = sb("identf", [128, 128], F32)
        P["identb"] = sb("identb", [128, 128], BF16)
        P["trif"] = sb("trif", [128, 128], F32)
        P["trib"] = sb("trib", [128, 128], BF16)
        arena_h = sb("arena", [128, 88 * 1024], U8)

        pid_sb = ctx.enter_context(nc.sbuf_tensor("pid_sb", [1, 1], U32))

        psF = [ctx.enter_context(nc.psum_tensor("psF%d" % i, [128, 1024], F32))
               for i in range(2)]
        psC = ctx.enter_context(nc.psum_tensor("psC", [128, 512], F32))
        psD = ctx.enter_context(nc.psum_tensor("psD", [128, 512], F32))
        psE = ctx.enter_context(nc.psum_tensor("psE", [128, 512], F32))
        psum = {
            "full": [psF[0][:, :], psF[1][:, :]],
            "half": [psC[:, :], psD[:, :]],
            "pv": psC[:, :],
            "sm": psD[:, :],
            "trb": [psF[0][:, 0:64].bitcast(BF16),
                    psF[0][:, 512:576].bitcast(BF16)],
            "trf": [psF[1][:, 0:128], psF[1][:, 512:640]],
            "g8": psC[:, 0:8],
            "bc": psE[:, :],
        }

        sems = {c: ctx.enter_context(nc.semaphore("sem_" + c))
                for c in CHANNELS}
        spid = ctx.enter_context(nc.semaphore("sem_pid"))

        nc.sync.dma_start(out=pid_sb[:], in_=nc.partition_id_tensor.ap()
                          ).then_inc(spid, 16)
        engines = [nc.sync, nc.tensor, nc.vector, nc.scalar, nc.gpsimd]
        idx = []
        for eng in engines:
            eng.wait_ge(spid, 16)
            val = eng.value_load(pid_sb[:1, :1])
            idx.append(val % 2)

        for i in nc.Switch(engines=engines, index=idx, n=2):
            sch = Sch()
            P["arena"] = Arena(arena_h, 88 * 1024)
            build_body(nc, sch, i, dram, P, psum)
            sch.plan()
            sch.emit(nc, sems)
    return nc


_NC_CACHE = {}


def kernel(L0_attn_x, L0_attn_W_kv_down, L0_attn_W_k_up, L0_attn_W_v_up,
           L0_attn_W_q, L0_attn_W_q_rope, L0_attn_W_k_rope, L0_attn_W_o,
           L0_ffn_gate_w, L0_ffn_expert_w):
    x = np.ascontiguousarray(np.asarray(L0_attn_x, dtype=np.float32))
    assert x.shape == (B_TOT, T, D)

    inv_freq = 1.0 / 10000.0 ** (np.arange(0, DR, 2, dtype=np.float64) / DR)
    tpos = np.arange(T, dtype=np.float64)
    freqs = np.outer(tpos, inv_freq)
    cos_t = np.cos(freqs).astype(np.float32).T
    sin_t = np.sin(freqs).astype(np.float32).T
    ident = np.eye(128, dtype=np.float32)
    kk, qq = np.meshgrid(np.arange(128), np.arange(128), indexing="ij")
    tri = (kk <= qq).astype(np.float32)

    weT = np.ascontiguousarray(
        np.asarray(L0_ffn_expert_w, dtype=np.float32).transpose(0, 2, 1))
    gwT = np.ascontiguousarray(np.asarray(L0_ffn_gate_w, dtype=np.float32).T)
    shared = {
        "wkvd": np.ascontiguousarray(np.asarray(L0_attn_W_kv_down, np.float32)),
        "wku": np.ascontiguousarray(np.asarray(L0_attn_W_k_up, np.float32)),
        "wvu": np.ascontiguousarray(np.asarray(L0_attn_W_v_up, np.float32)),
        "wq": np.ascontiguousarray(np.asarray(L0_attn_W_q, np.float32)),
        "wqr": np.ascontiguousarray(np.asarray(L0_attn_W_q_rope, np.float32)),
        "wkr": np.ascontiguousarray(np.asarray(L0_attn_W_k_rope, np.float32)),
        "wo": np.ascontiguousarray(np.asarray(L0_attn_W_o, np.float32)),
        "gwT": gwT, "weT": weT, "ident": ident, "tri": tri,
    }
    cosF, sinF = [], []
    for vpar in range(2):
        order = PERM[vpar]
        cosF.append(np.ascontiguousarray(np.concatenate(
            [np.tile(cos_t[:, 1024 * qt:1024 * qt + 1024], (2, 1))
             for qt in order], axis=1)))
        sinF.append(np.ascontiguousarray(np.concatenate(
            [np.tile(sin_t[:, 1024 * qt:1024 * qt + 1024], (2, 1))
             for qt in order], axis=1)))

    if "nc" not in _NC_CACHE:
        _NC_CACHE["nc"] = build_nc()
    nc = _NC_CACHE["nc"]

    in_maps = []
    for core in range(8):
        vpar = core % 2
        seq = core // 2
        xs = x[seq].reshape(4, 1024, D)[PERM[vpar]].reshape(T, D)
        m = dict(shared)
        m["xseq"] = np.ascontiguousarray(xs)
        m["cosF"] = cosF[vpar]
        m["sinF"] = sinF[vpar]
        in_maps.append(m)

    res = run_bass_kernel_spmd(nc, in_maps, core_ids=list(range(8)))

    out = np.empty((B_TOT, T, D), dtype=np.float32)
    for core in range(8):
        vpar = core % 2
        seq = core // 2
        oc = np.asarray(res.results[core]["out"])
        q0, q1 = PERM[vpar][0], PERM[vpar][1]
        out[seq, 1024 * q0:1024 * q0 + 1024] = oc[0:1024]
        out[seq, 1024 * q1:1024 * q1 + 1024] = oc[1024:2048]
    return out


# version 10
# speedup vs baseline: 1.0411x; 1.0411x over previous
import math
import contextlib
import numpy as np

import concourse.bass as bass
import concourse.mybir as mybir
from concourse.bass_utils import run_bass_kernel_spmd

F32 = mybir.dt.float32
BF16 = mybir.dt.bfloat16
U32 = mybir.dt.uint32
U8 = mybir.dt.uint8
AF = mybir.ActivationFunctionType
ALU = mybir.AluOpType
AX = mybir.AxisListType

B_TOT = 4
D = 1024
T = 4096
NH = 16
HD = 64
DC = 128
DR = 32
NTB = 32
SCALE = 1.0 / math.sqrt(HD + DR)
CSH = 12.0
EPS = 1e-5

PERM = [[0, 3, 1, 2], [1, 2, 0, 3]]
SLOT = [[0, 2, 3, 1], [2, 0, 1, 3]]

CHANNELS = ("pe", "act", "dve", "pool", "sp", "dma")
ENG_OF = {"pe": "tensor", "act": "scalar", "dve": "vector", "pool": "gpsimd",
          "sp": "sync", "dma": "sync"}


class Sch:

    def __init__(self):
        self.ops = []
        self.res = {}
        self.nbar = 0
        self.last_real = {}

    def op(self, ch, fn, r=(), w=(), pr=(), tag=""):
        idx = len(self.ops)
        if tag != "bar":
            self.last_real[ch] = idx
        deps = {}

        def dep(j, ptr):
            if j is not None:
                deps[j] = deps.get(j, False) or ptr

        for name in r:
            e = self.res.get(name)
            if e:
                dep(e["w"], False)
                if name.startswith("ps"):
                    for j in e["r"]:
                        dep(j, False)
        for name in pr:
            e = self.res.get(name)
            if e:
                dep(e["w"], True)
        for name in w:
            e = self.res.get(name)
            if e:
                dep(e["w"], False)
                for j in e["r"]:
                    dep(j, False)
        deps.pop(idx, None)
        self.ops.append({"ch": ch, "fn": fn, "deps": deps,
                         "inc": ch == "dma", "tag": tag})
        for name in list(r) + list(pr):
            self.res.setdefault(name, {"w": None, "r": []})["r"].append(idx)
        for name in w:
            self.res[name] = {"w": idx, "r": []}
        return idx

    def dma(self, fn, r=(), w=(), tag=""):
        return self.op("dma", fn, r=r, w=list(w) + ["_dmalast"], tag=tag)

    def barrier(self):
        snap = dict(self.last_real)
        for ch in ("pe", "act", "dve", "pool", "sp"):
            deps = {j: False for c2, j in snap.items() if c2 != ch}
            self.ops.append({"ch": ch, "fn": lambda e: e.nop(),
                             "deps": deps, "inc": False, "tag": "bar"})

    def plan(self):
        for o in self.ops:
            need = {}
            for j, is_ptr in o["deps"].items():
                p = self.ops[j]
                if p["ch"] == o["ch"] and p["ch"] != "dma" and not is_ptr:
                    continue
                p["inc"] = True
                need[j] = True
            o["need"] = sorted(need)
        cnt = {c: 0 for c in CHANNELS}
        for o in self.ops:
            if o["inc"]:
                cnt[o["ch"]] += 16 if o["ch"] == "dma" else 1
                o["val"] = cnt[o["ch"]]

    def emit(self, nc, sems):
        hw = {c: {} for c in CHANNELS}
        for o in self.ops:
            eng = getattr(nc, ENG_OF[o["ch"]])
            ech = "sp" if o["ch"] == "dma" else o["ch"]
            want = {}
            for j in o["need"]:
                p = self.ops[j]
                want[p["ch"]] = max(want.get(p["ch"], 0), p["val"])
            for pch, val in want.items():
                if hw[ech].get(pch, 0) >= val:
                    continue
                eng.wait_ge(sems[pch], val)
                hw[ech][pch] = val
            inst = o["fn"](eng)
            if o["inc"]:
                inst.then_inc(sems[o["ch"]], 16 if o["ch"] == "dma" else 1)


class Arena:

    def __init__(self, handle, nbytes):
        self.h = handle
        self.n = nbytes
        self.off = 0
        self.marks = []

    def push(self):
        self.marks.append(self.off)

    def pop(self):
        self.off = self.marks.pop()

    def take(self, nbytes, dt, parts=128):
        off = (self.off + 63) & ~63
        self.off = off + nbytes
        assert self.off <= self.n, "arena overflow: %d > %d" % (self.off, self.n)
        ap = self.h[0:parts, off:off + nbytes]
        return ap.bitcast(dt) if dt != U8 else ap

    def f32(self, cols, parts=128):
        return self.take(cols * 4, F32, parts)

    def bf16(self, cols, parts=128):
        return self.take(cols * 2, BF16, parts)


PSF = [["psF0a", "psF0b"], ["psF1a", "psF1b"]]


def build_body(nc, sch, v, dram, P, psum):
    ar = P["arena"]

    sch.op("dve", lambda e: e.memset(P["negC"][:], -CSH), w=["negC"])
    sch.op("dve", lambda e: e.memset(P["zero1"][:], 0.0), w=["zero1"])
    sch.op("dve", lambda e: e.memset(P["eps"][:], EPS), w=["eps"])
    sch.op("dve", lambda e: e.memset(P["ones64"][:], 1.0), w=["ones64"])
    sch.dma(lambda e: e.dma_start(out=P["identf"][:], in_=dram["ident"]),
            w=["identf"])
    sch.dma(lambda e: e.dma_start(out=P["trif"][:], in_=dram["tri"]), w=["trif"])
    sch.op("pool", lambda e: e.tensor_copy(P["identb"][:], P["identf"][:]),
           r=["identf"], w=["identb"])
    sch.op("pool", lambda e: e.tensor_copy(P["trib"][:], P["trif"][:]),
           r=["trif"], w=["trib"])
    for c in range(8):
        sch.dma(lambda e, c=c: e.dma_start(
            out=P["gwT"][:, 8 * c:8 * c + 8],
            in_=dram["gwT"][128 * c:128 * c + 128, :]), w=["gwT%d" % c])

    ar.push()
    xn_own = ar.bf16(8 * 2048)
    ar.push()
    xn_oth = ar.bf16(8 * 2048)
    ar.push()
    xst = [ar.f32(D) for _ in range(2)]
    xnt = [ar.f32(D) for _ in range(2)]
    xnb = [ar.bf16(D) for _ in range(2)]
    xT = [ar.f32(128) for _ in range(4)]
    g_all = ar.f32(16 * 8)
    statw = ar.f32(16)
    mall = ar.f32(16)
    gem = ar.f32(8 * 16)
    tkk = ar.f32(16)
    uu1 = ar.f32(16)

    def xnf(c, tb):
        return "xnf.%d.%d" % (c, tb // 4)

    def p1_front2(tp):
        for r in range(2):
            tb = 2 * tp + r
            sch.dma(lambda e, tb=tb, r=r: e.dma_start(
                out=xst[r], in_=dram["xseq"][128 * tb:128 * tb + 128, :]),
                w=["xst%d" % r], tag="ldx")
            sch.op("dve", lambda e, r=r: e.reduce_sum(
                statw[:, 0 + r:1 + r], xst[r], axis=AX.X),
                r=["xst%d" % r], w=["stat_s%d" % r])
            sch.op("act", lambda e, r=r: e.activation(
                xnt[r], xst[r], AF.Square, accum_out=statw[:, 2 + r:3 + r]),
                r=["xst%d" % r], w=["stat_q%d" % r, "xnt%d" % r])
        sch.op("dve", lambda e: e.tensor_scalar(
            statw[:, 4:6], statw[:, 0:2], 1.0 / D, None, ALU.mult),
            pr=["stat_s0", "stat_s1"], w=["stat_m0", "stat_m1"])
        sch.op("dve", lambda e: e.tensor_mul(
            statw[:, 6:8], statw[:, 4:6], statw[:, 4:6]),
            pr=["stat_m0", "stat_m1"], w=["stat_m2"])
        sch.op("dve", lambda e: e.scalar_tensor_tensor(
            statw[:, 8:10], statw[:, 2:4], 1.0 / D, statw[:, 6:8],
            ALU.mult, ALU.subtract),
            pr=["stat_q0", "stat_q1", "stat_m2"], w=["stat_v"])
        sch.op("act", lambda e: e.activation(
            statw[:, 10:12], statw[:, 8:10], AF.Ln, bias=P["eps"][:, 0:1],
            scale=1.0),
            pr=["eps", "stat_v"], w=["stat_lv"])
        sch.op("act", lambda e: e.activation(
            statw[:, 12:14], statw[:, 10:12], AF.Exp, bias=P["zero1"][:, 0:1],
            scale=-0.5),
            pr=["zero1", "stat_lv"], w=["stat_r0", "stat_r1"])

    def p1_back(tb):
        r = tb % 2
        own = tb < 16
        sch.op("dve", lambda e, r=r: e.tensor_scalar(
            xnt[r], xst[r], statw[:, 4 + r:5 + r], statw[:, 12 + r:13 + r],
            ALU.subtract, ALU.mult),
            r=["xst%d" % r], pr=["stat_m%d" % r, "stat_r%d" % r],
            w=["xnt%d" % r])
        if own:
            sch.op("dve", lambda e, tb=tb, r=r: e.tensor_add(
                P["s"][:, D * tb:D * tb + D], xst[r], xnt[r]),
                r=["xst%d" % r, "xnt%d" % r], w=["s.%d" % tb])
        sch.op("dve", lambda e, r=r: e.tensor_copy(xnb[r], xnt[r]),
               r=["xnt%d" % r], w=["xnb%d" % r])
        for c in range(8):
            p2 = c % 2
            pst = psum["trb"][p2]
            pn = PSF[0][p2]
            if own:
                dst, col = xn_own, 2048 * c + 128 * tb
            else:
                dst, col = xn_oth, 2048 * c + 128 * (tb - 16)
            sch.op("pe", lambda e, c=c, r=r, p=pst: e.transpose(
                p, xnb[r][:, 128 * c:128 * c + 128], P["identb"][:]),
                r=["xnb%d" % r, "identb"], w=[pn], tag="trb")
            sch.op("act", lambda e, p=pst, d=dst, col=col: e.copy(
                d[:, col:col + 128], p),
                r=[pn], w=[xnf(c, tb)])
        if own:
            for half in range(2):
                for ci in range(4):
                    c = 4 * half + ci
                    p2 = c % 2
                    pst = psum["trf"][p2]
                    pn = PSF[1][p2]
                    sch.op("pe", lambda e, r=r, c=c, p=pst: e.transpose(
                        p, xnt[r][:, 128 * c:128 * c + 128], P["identf"][:]),
                        r=["xnt%d" % r, "identf"], w=[pn], tag="trf")
                    sch.op("act", lambda e, p=pst, ci=ci: e.copy(xT[ci], p),
                           r=[pn], w=["xT%d" % ci])

                def gfn(e, half=half):
                    for ci in range(4):
                        c = 4 * half + ci
                        i = e.matmul(psum["g8"], xT[ci],
                                     P["gwT"][:, 8 * c:8 * c + 8],
                                     start=(c == 0), stop=(c == 7))
                    return i
                sch.op("pe", gfn,
                       r=["xT%d" % ci for ci in range(4)] +
                         ["gwT%d" % c for c in range(8)] + ["psC"],
                       w=["psC"], tag="gate")
            sch.op("dve", lambda e, tb=tb: e.tensor_copy(
                g_all[:, 8 * tb:8 * tb + 8], psum["g8"]),
                r=["psC"], w=["gall%d" % tb])

    for tp in range(NTB // 2):
        p1_front2(tp)
        p1_back(2 * tp)
        p1_back(2 * tp + 1)

    gv = g_all.rearrange("p (t e) -> p t e", t=16)
    gemv = gem.rearrange("p (e t) -> p e t", e=8)
    mgv = P["masksg"][:].rearrange("p (e t) -> p e t", e=8)
    gall = ["gall%d" % t for t in range(16)]
    sch.op("dve", lambda e: e.reduce_max(mall[:, :], gv, axis=AX.X),
           r=gall, w=["mall"])
    sch.op("dve", lambda e: e.tensor_copy(gemv, gv.transpose([0, 2, 1])),
           r=gall, pr=["mall"], w=["gem"])
    mbc = bass.AP(tensor=mall.tensor, offset=mall.offset,
                  ap=[mall.ap[0], [0, 8], [1, 16]])
    sch.op("dve", lambda e: e.tensor_tensor(mgv, gemv, mbc, ALU.is_equal),
           pr=["gem", "mall"], w=["raw"])
    sch.op("dve", lambda e: e.memset(tkk, 0.0), w=["tk"])
    for ei in range(8):
        sch.op("dve", lambda e: e.tensor_scalar(
            uu1, tkk, -1.0, 1.0, ALU.mult, ALU.add), pr=["tk"], w=["uu"])
        sch.op("dve", lambda e, ei=ei: e.tensor_mul(
            mgv[:, ei, :], mgv[:, ei, :], uu1),
            pr=["raw", "uu"], w=["mask%d" % ei, "raw"])
        sch.op("dve", lambda e, ei=ei: e.tensor_add(tkk, tkk, mgv[:, ei, :]),
               pr=["mask%d" % ei, "tk"], w=["tk"])
    sch.barrier()
    ar.pop()

    ar.push()
    wkst = ar.f32(8 * 128)
    wkb = ar.bf16(8 * 128)
    for c in range(8):
        sch.dma(lambda e, c=c: e.dma_start(
            out=wkst[:, 128 * c:128 * c + 128],
            in_=dram["wkvd"][128 * c:128 * c + 128, :]), w=["wkst%d" % c])
        sch.op("pool", lambda e, c=c: e.tensor_copy(
            wkb[:, 128 * c:128 * c + 128], wkst[:, 128 * c:128 * c + 128]),
            r=["wkst%d" % c], w=["wkb%d" % c])
    for b5 in range(8):
        pp = psum["half"][b5 % 2]
        ppn = "psC" if b5 % 2 == 0 else "psD"
        xn_src = xn_own if b5 < 4 else xn_oth
        bl = b5 % 4

        def fn(e, pp=pp, xn_src=xn_src, bl=bl):
            for c in range(8):
                i = e.matmul(pp, wkb[:, 128 * c:128 * c + 128],
                             xn_src[:, 2048 * c + 512 * bl:2048 * c + 512 * bl + 512],
                             start=(c == 0), stop=(c == 7))
            return i
        sch.op("pe", fn,
               r=["wkb%d" % c for c in range(8)] +
                 ["xnf.%d.%d" % (c, b5) for c in range(8)] + [ppn],
               w=[ppn], tag="ckv")
        sch.op("act", lambda e, pp=pp, b5=b5: e.copy(
            P["ckv"][:, 512 * b5:512 * b5 + 512], pp),
            r=[ppn], w=["ckv%d" % b5])
    sch.barrier()
    ar.pop()
    ar.pop()

    ar.push()
    est = [ar.f32(D) for _ in range(4)]
    ebf = [[ar.bf16(D) for _ in range(8)] for _ in range(2)]
    for ei in range(8):
        er = ei % 2
        for hc in range(8):
            sr = (ei * 8 + hc) % 4
            sch.dma(lambda e, ei=ei, hc=hc, sr=sr: e.dma_start(
                out=est[sr], in_=dram["weT"][ei, 128 * hc:128 * hc + 128, :]),
                w=["est%d" % sr], tag="ldw_e")
            sch.op("act", lambda e, er=er, hc=hc, sr=sr: e.copy(
                ebf[er][hc], est[sr]),
                r=["est%d" % sr], w=["ebf%d_%d" % (er, hc)])
        for tb in range(16):
            pp = psum["full"][tb % 2]
            pns = PSF[tb % 2]

            def fn(e, er=er, tb=tb, pp=pp):
                for hc in range(8):
                    lhs = xn_own[:, 2048 * hc + 128 * tb:2048 * hc + 128 * tb + 128]
                    e.matmul(pp[:, 0:512], lhs, ebf[er][hc][:, 0:512],
                             start=(hc == 0), stop=(hc == 7))
                    i = e.matmul(pp[:, 512:1024], lhs, ebf[er][hc][:, 512:1024],
                                 start=(hc == 0), stop=(hc == 7))
                return i
            sch.op("pe", fn,
                   r=["ebf%d_%d" % (er, hc) for hc in range(8)] +
                     ["xnf.%d.%d" % (c, tb // 4) for c in range(8)] + pns,
                   w=pns, tag="ffn")
            for hf in range(2):
                sch.op("dve", lambda e, tb=tb, ei=ei, hf=hf, pp=pp:
                       e.scalar_tensor_tensor(
                           P["s"][:, D * tb + 512 * hf:D * tb + 512 * hf + 512],
                           pp[:, 512 * hf:512 * hf + 512],
                           mgv[:, ei, tb:tb + 1],
                           P["s"][:, D * tb + 512 * hf:D * tb + 512 * hf + 512],
                           ALU.mult, ALU.add),
                       r=[pns[hf], "s.%d" % tb], pr=["mask%d" % ei],
                       w=["s.%d" % tb], tag="ffn_ev")
    sch.barrier()
    ar.pop()

    ar.push()
    wqst = [ar.f32(D) for _ in range(2)]
    wqb = [ar.bf16(D) for _ in range(8)]
    wrst = ar.f32(8 * 32)
    wrot = ar.f32(8 * 32)
    wkst2 = ar.f32(32)
    wkrot = ar.f32(32)
    wqrb = ar.bf16(8 * 32)
    wqrrb = ar.bf16(8 * 32)
    wkrb = ar.bf16(64)
    cst = [ar.f32(512, parts=32) for _ in range(2)]
    sst = [ar.f32(512, parts=32) for _ in range(2)]
    t1 = ar.f32(512, parts=32)
    t2 = ar.f32(512, parts=32)

    for c in range(8):
        r = c % 2
        sch.dma(lambda e, c=c, r=r: e.dma_start(
            out=wqst[r], in_=dram["wq"][128 * c:128 * c + 128, :]),
            w=["wqst%d" % r], tag="ldwq")
        sch.op("act", lambda e, c=c, r=r: e.mul(wqb[c], wqst[r], SCALE),
            r=["wqst%d" % r], w=["wqb%d" % c])
    for do in range(8):
        for bl in range(4):
            pp = psum["half"][bl % 2]
            ppn = "psC" if bl % 2 == 0 else "psD"

            def fn(e, do=do, bl=bl, pp=pp):
                for c in range(8):
                    i = e.matmul(
                        pp, wqb[c][:, 128 * do:128 * do + 128],
                        xn_own[:, 2048 * c + 512 * bl:2048 * c + 512 * bl + 512],
                        start=(c == 0), stop=(c == 7))
                return i
            sch.op("pe", fn,
                   r=["wqb%d" % c for c in range(8)] +
                     ["xnf.%d.%d" % (c, bl) for c in range(8)] + [ppn],
                   w=[ppn], tag="q")
            sch.op("act", lambda e, pp=pp, do=do, bl=bl: e.copy(
                P["q"][:, 2048 * do + 512 * bl:2048 * do + 512 * bl + 512], pp),
                r=[ppn], w=["q.%d.%d" % (do, bl)])

    sch.dma(lambda e: e.dma_start(out=wkst2[:, 0:32], in_=dram["wkr"]),
            w=["wkrst"])
    sch.op("dve", lambda e: e.tensor_scalar(
        wkrot[:, 0:16], wkst2[:, 16:32], -1.0, None, ALU.mult),
        r=["wkrst"], w=["wkro"])
    sch.op("dve", lambda e: e.tensor_copy(wkrot[:, 16:32], wkst2[:, 0:16]),
           r=["wkrst"], w=["wkro2"])
    sch.op("pool", lambda e: e.tensor_copy(wkrb[:, 0:32], wkst2[:, 0:32]),
           r=["wkrst"], w=["wkrb"])
    sch.op("pool", lambda e: e.tensor_copy(wkrb[:, 32:64], wkrot[:, 0:32]),
           r=["wkro", "wkro2"], w=["wkrb2"])
    for c in range(8):
        sch.dma(lambda e, c=c: e.dma_start(
            out=wrst[:, 32 * c:32 * c + 32],
            in_=dram["wqr"][128 * c:128 * c + 128, :]), w=["wqrst%d" % c])
        sch.op("dve", lambda e, c=c: e.tensor_scalar(
            wrot[:, 32 * c:32 * c + 16], wrst[:, 32 * c + 16:32 * c + 32],
            -1.0, None, ALU.mult), r=["wqrst%d" % c], w=["wqro%d" % c])
        sch.op("dve", lambda e, c=c: e.tensor_copy(
            wrot[:, 32 * c + 16:32 * c + 32], wrst[:, 32 * c:32 * c + 16]),
            r=["wqrst%d" % c], w=["wqro2%d" % c])
        sch.op("dve", lambda e, c=c: e.tensor_scalar(
            wqrb[:, 32 * c:32 * c + 32], wrst[:, 32 * c:32 * c + 32],
            SCALE, None, ALU.mult), r=["wqrst%d" % c], w=["wqrb%d" % c])
        sch.op("dve", lambda e, c=c: e.tensor_scalar(
            wqrrb[:, 32 * c:32 * c + 32], wrot[:, 32 * c:32 * c + 32],
            SCALE, None, ALU.mult),
            r=["wqro%d" % c, "wqro2%d" % c], w=["wqrrb%d" % c])

    for b5 in range(8):
        ppa, ppb = psum["half"][0], psum["half"][1]
        cr = b5 % 2
        sch.dma(lambda e, b5=b5, cr=cr: e.dma_start(
            out=cst[cr], in_=dram["cosF"][:, 512 * b5:512 * b5 + 512]),
            w=["cst%d" % cr])
        sch.dma(lambda e, b5=b5, cr=cr: e.dma_start(
            out=sst[cr], in_=dram["sinF"][:, 512 * b5:512 * b5 + 512]),
            w=["sst%d" % cr])
        sch.op("pe", lambda e, b5=b5, p=ppa: e.matmul(
            p[0:32, :], wkrb[:, 0:32], P["ckv"][:, 512 * b5:512 * b5 + 512],
            start=True, stop=True),
            r=["wkrb", "ckv%d" % b5, "psC"], w=["psC"], tag="kr")
        sch.op("pe", lambda e, b5=b5, p=ppb: e.matmul(
            p[0:32, :], wkrb[:, 32:64], P["ckv"][:, 512 * b5:512 * b5 + 512],
            start=True, stop=True),
            r=["wkrb2", "ckv%d" % b5, "psD"], w=["psD"], tag="krr")
        sch.op("dve", lambda e, cr=cr, p=ppa: e.tensor_tensor(
            t1[0:32, :], p[0:32, :], cst[cr], ALU.mult),
            r=["psC", "cst%d" % cr], w=["t1"])
        sch.op("dve", lambda e, cr=cr, p=ppb: e.tensor_tensor(
            t2[0:32, :], p[0:32, :], sst[cr], ALU.mult),
            r=["psD", "sst%d" % cr], w=["t2"])
        sch.op("dve", lambda e, b5=b5: e.tensor_add(
            P["kr"][0:32, 512 * b5:512 * b5 + 512], t1[0:32, :], t2[0:32, :]),
            r=["t1", "t2"], w=["kr.%d" % b5])
        sch.op("dve", lambda e, b5=b5: e.tensor_add(
            P["kr"][64:96, 512 * b5:512 * b5 + 512], t1[0:32, :], t2[0:32, :]),
            r=["t1", "t2"], w=["krd.%d" % b5])

    for bl in range(4):
        ppa, ppb = psum["half"][0], psum["half"][1]
        cr = bl % 2

        def fq(e, bl=bl, p=ppa):
            for c in range(8):
                i = e.matmul(
                    p[0:32, :], wqrb[:, 32 * c:32 * c + 32],
                    xn_own[:, 2048 * c + 512 * bl:2048 * c + 512 * bl + 512],
                    start=(c == 0), stop=(c == 7))
            return i

        def fqr(e, bl=bl, p=ppb):
            for c in range(8):
                i = e.matmul(
                    p[0:32, :], wqrrb[:, 32 * c:32 * c + 32],
                    xn_own[:, 2048 * c + 512 * bl:2048 * c + 512 * bl + 512],
                    start=(c == 0), stop=(c == 7))
            return i
        sch.op("pe", fq, r=["wqrb%d" % c for c in range(8)] +
               ["xnf.%d.%d" % (c, bl) for c in range(8)] + ["psC"],
               w=["psC"], tag="qr")
        sch.op("pe", fqr, r=["wqrrb%d" % c for c in range(8)] +
               ["xnf.%d.%d" % (c, bl) for c in range(8)] + ["psD"],
               w=["psD"], tag="qrr")
        sch.dma(lambda e, bl=bl, cr=cr: e.dma_start(
            out=cst[cr], in_=dram["cosF"][:, 512 * bl:512 * bl + 512]),
            w=["cst%d" % cr])
        sch.dma(lambda e, bl=bl, cr=cr: e.dma_start(
            out=sst[cr], in_=dram["sinF"][:, 512 * bl:512 * bl + 512]),
            w=["sst%d" % cr])
        sch.op("dve", lambda e, cr=cr, p=ppa: e.tensor_tensor(
            t1[0:32, :], p[0:32, :], cst[cr], ALU.mult),
            r=["psC", "cst%d" % cr], w=["t1"])
        sch.op("dve", lambda e, cr=cr, p=ppb: e.tensor_tensor(
            t2[0:32, :], p[0:32, :], sst[cr], ALU.mult),
            r=["psD", "sst%d" % cr], w=["t2"])
        sch.op("dve", lambda e, bl=bl: e.tensor_add(
            P["qr"][0:32, 512 * bl:512 * bl + 512], t1[0:32, :], t2[0:32, :]),
            r=["t1", "t2"], w=["qr.%d" % bl])
        sch.op("dve", lambda e, bl=bl: e.tensor_add(
            P["qr"][64:96, 512 * bl:512 * bl + 512], t1[0:32, :], t2[0:32, :]),
            r=["t1", "t2"], w=["qrd.%d" % bl])
    sch.barrier()
    ar.pop()
    ar.pop()

    ar.push()
    wost = [ar.f32(D) for _ in range(2)]
    wob = [ar.bf16(D) for _ in range(8)]
    wust = ar.f32(D)
    wub = ar.bf16(D)
    wvst = ar.f32(D)
    wvb = ar.bf16(D)
    kq = [ar.bf16(T) for _ in range(2)]
    vq = ar.bf16(32 * 256)
    pt = [ar.bf16(1024) for _ in range(2)]
    afm = [[ar.bf16(512) for _ in range(2)] for _ in range(2)]
    rcp = [ar.f32(512) for _ in range(2)]

    for c in range(8):
        r = c % 2
        sch.dma(lambda e, c=c, r=r: e.dma_start(
            out=wost[r], in_=dram["wo"][128 * c:128 * c + 128, :]),
            w=["wost%d" % r], tag="ldwo")
        sch.op("dve", lambda e, c=c, r=r: e.tensor_copy(wob[c], wost[r]),
               r=["wost%d" % r], w=["wob%d" % c])
    sch.dma(lambda e: e.dma_start(out=wust, in_=dram["wku"]), w=["wust"])
    sch.op("dve", lambda e: e.tensor_copy(wub, wust), r=["wust"], w=["wub"])
    sch.dma(lambda e: e.dma_start(out=wvst, in_=dram["wvu"]), w=["wvst"])
    sch.op("dve", lambda e: e.tensor_copy(wvb, wvst), r=["wvst"], w=["wvb"])

    slot = SLOT[v]
    perm = PERM[v]

    for pp4 in range(4):
        for dl in range(2):
            do = 2 * pp4 + dl
            for b5 in range(8):
                pb = psum["half"][b5 % 2]
                ppn = "psC" if b5 % 2 == 0 else "psD"
                sch.op("pe", lambda e, do=do, b5=b5, pb=pb: e.matmul(
                    pb, wub[:, 128 * do:128 * do + 128],
                    P["ckv"][:, 512 * b5:512 * b5 + 512],
                    start=True, stop=True),
                    r=["wub", "ckv%d" % b5, ppn], w=[ppn], tag="kup")
                sch.op("dve", lambda e, dl=dl, b5=b5, pb=pb: e.tensor_copy(
                    kq[dl][:, 512 * b5:512 * b5 + 512], pb),
                    r=[ppn], w=["kq.%d.%d" % (dl, b5)])
        for tb in range(NTB):
            pb = psum["half"][tb % 2]
            ppn = "psC" if tb % 2 == 0 else "psD"
            sch.op("pe", lambda e, tb=tb, pp4=pp4, pb=pb: e.matmul(
                pb[:, 0:256], P["ckv"][:, 128 * tb:128 * tb + 128],
                wvb[:, 256 * pp4:256 * pp4 + 256], start=True, stop=True),
                r=["wvb", "ckv%d" % (tb // 4), ppn], w=[ppn], tag="vup")
            sch.op("dve", lambda e, tb=tb, pb=pb: e.tensor_copy(
                vq[:, 256 * tb:256 * tb + 256], pb[:, 0:256]),
                r=[ppn], w=["vq.%d" % tb])

        for g in range(4):
            gt = 1024 * perm[g // 2] + 512 * (g % 2)
            nrect = gt // 128
            kcs = [(8 * slot[kc // 8] + kc % 8, None) for kc in range(nrect)]
            kcs += [(4 * g + j, j) for j in range(4)]
            n_kc = len(kcs)
            for hl in range(2):
                hp = 2 * pp4 + hl
                dl = hl
                pend = []
                for ik, (sc, dj) in enumerate(kcs):
                    ps = psum["full"][ik % 2]
                    pns = PSF[ik % 2]
                    ptb = pt[ik % 2]
                    ptn = "pt%d" % (ik % 2)
                    q0 = 0 if dj is None else 128 * dj
                    first = ik == 0
                    last = ik == n_kc - 1

                    def smm(e, sc=sc, dl=dl, hp=hp, g=g, ps=ps, q0=q0):
                        kcol = 128 * sc
                        qc = 2048 * hp + 512 * g + q0
                        e.matmul(ps[:, q0:512],
                                 kq[dl][0:64, kcol:kcol + 128],
                                 P["q"][0:64, qc:qc + 512 - q0],
                                 start=True, stop=False,
                                 tile_position=(0, 0))
                        e.matmul(ps[:, 512 + q0:1024],
                                 kq[dl][64:128, kcol:kcol + 128],
                                 P["q"][64:128, qc:qc + 512 - q0],
                                 start=True, stop=False,
                                 tile_position=(64, 0))
                        e.matmul(ps[:, q0:512],
                                 P["kr"][0:32, kcol:kcol + 128],
                                 P["qr"][0:32, 512 * g + q0:512 * g + 512],
                                 start=False, stop=True,
                                 tile_position=(0, 0))
                        i = e.matmul(ps[:, 512 + q0:1024],
                                     P["kr"][64:96, kcol:kcol + 128],
                                     P["qr"][64:96, 512 * g + q0:512 * g + 512],
                                     start=False, stop=True,
                                     tile_position=(64, 0))
                        return i
                    sch.op("pe", smm,
                           r=["kq.%d.%d" % (dl, sc // 4), "q.%d.%d" % (hp, g),
                              "kr.%d" % (sc // 4), "krd.%d" % (sc // 4),
                              "qr.%d" % g, "qrd.%d" % g] + pns,
                           w=pns, tag="S")

                    if dj is None:
                        sch.op("act", lambda e, ps=ps, ptb=ptb: e.activation(
                            ptb, ps, AF.Exp, bias=P["negC"][:, 0:1], scale=1.0),
                            r=pns, pr=["negC"], w=[ptn], tag="exp")
                    else:
                        def efn(e, ps=ps, ptb=ptb, q0=q0):
                            e.activation(ptb[:, q0:512], ps[:, q0:512], AF.Exp,
                                         bias=P["negC"][:, 0:1], scale=1.0)
                            return e.activation(
                                ptb[:, 512 + q0:1024], ps[:, 512 + q0:1024],
                                AF.Exp, bias=P["negC"][:, 0:1], scale=1.0)
                        sch.op("act", efn, r=pns, pr=["negC"], w=[ptn],
                               tag="expd")

                        def mfn(e, ptb=ptb, q0=q0):
                            e.tensor_mul(ptb[:, q0:q0 + 128],
                                         ptb[:, q0:q0 + 128], P["trib"][:])
                            return e.tensor_mul(
                                ptb[:, 512 + q0:512 + q0 + 128],
                                ptb[:, 512 + q0:512 + q0 + 128], P["trib"][:])
                        sch.op("dve", mfn, r=[ptn, "trib"], w=[ptn],
                               tag="trimask")

                    def pvfn(e, sc=sc, hl=hl, ptb=ptb, q0=q0,
                             first=first, last=last):
                        vcol = 256 * sc + 128 * hl
                        e.matmul(psum["pv"][0:64, q0:512],
                                 vq[:, vcol:vcol + 64], ptb[:, q0:512],
                                 start=first, stop=last,
                                 tile_position=(0, 0))
                        e.matmul(psum["pv"][64:128, q0:512],
                                 vq[:, vcol + 64:vcol + 128],
                                 ptb[:, 512 + q0:1024],
                                 start=first, stop=last,
                                 tile_position=(0, 64))
                        e.matmul(psum["sm"][0:64, q0:512],
                                 P["ones64"][:], ptb[:, q0:512],
                                 start=first, stop=last,
                                 tile_position=(0, 0))
                        i = e.matmul(psum["sm"][64:128, q0:512],
                                     P["ones64"][:], ptb[:, 512 + q0:1024],
                                     start=first, stop=last,
                                     tile_position=(0, 64))
                        return i
                    if pend:
                        pf, pr_ = pend.pop()
                        sch.op("pe", pf, r=pr_, w=["psC", "psD"], tag="pv")
                    pend.append((pvfn, ["vq.%d" % sc, ptn, "ones64",
                                        "psC", "psD"]))
                if pend:
                    pf, pr_ = pend.pop()
                    sch.op("pe", pf, r=pr_, w=["psC", "psD"], tag="pv")

                ring = g % 2
                sch.op("dve", lambda e, ring=ring: e.reciprocal(
                    rcp[ring], psum["sm"]), r=["psD"], w=["rcp%d" % ring])
                sch.op("dve", lambda e, ring=ring, hl=hl: e.tensor_tensor(
                    afm[ring][hl], psum["pv"], rcp[ring], ALU.mult),
                    r=["psC", "rcp%d" % ring], w=["afm.%d.%d" % (ring, hl)],
                    tag="norm")

            ring = g % 2
            for qt in range(4):
                pw = psum["full"][qt % 2]
                pns = PSF[qt % 2]

                def wofn(e, pp4=pp4, ring=ring, qt=qt, pw=pw):
                    for dl in range(2):
                        a = afm[ring][dl][:, 128 * qt:128 * qt + 128]
                        wo_c = wob[2 * pp4 + dl]
                        e.matmul(pw[:, 0:512], a, wo_c[:, 0:512],
                                 start=(dl == 0), stop=(dl == 1))
                        i = e.matmul(pw[:, 512:1024], a, wo_c[:, 512:1024],
                                     start=(dl == 0), stop=(dl == 1))
                    return i
                sch.op("pe", wofn,
                       r=["afm.%d.0" % ring, "afm.%d.1" % ring,
                          "wob%d" % (2 * pp4), "wob%d" % (2 * pp4 + 1)] + pns,
                       w=pns, tag="wo")
                tbq = 4 * g + qt
                sch.op("dve", lambda e, tbq=tbq, pw=pw: e.tensor_add(
                    P["s"][:, D * tbq:D * tbq + D],
                    P["s"][:, D * tbq:D * tbq + D], pw),
                    r=pns + ["s.%d" % tbq], w=["s.%d" % tbq], tag="wo_ev")
    sch.barrier()
    ar.pop()

    outv = dram["out"].rearrange("(t p) d -> p t d", p=128)
    sv = P["s"][:, :].rearrange("p (t d) -> p t d", t=16)
    sch.dma(lambda e: e.dma_start(out=outv, in_=sv),
            r=["s.%d" % t for t in range(16)], w=["outdma"], tag="stout")
    sch.op("sp", lambda e: e.nop(), r=["_dmalast"], tag="final")


def build_nc():
    nc = bass.Bass("TRN2", target_bir_lowering=False)
    dram = {
        "xseq": nc.dram_tensor("xseq", [T, D], F32, kind="ExternalInput").ap(),
        "wkvd": nc.dram_tensor("wkvd", [D, DC], F32, kind="ExternalInput").ap(),
        "wku": nc.dram_tensor("wku", [DC, D], F32, kind="ExternalInput").ap(),
        "wvu": nc.dram_tensor("wvu", [DC, D], F32, kind="ExternalInput").ap(),
        "wq": nc.dram_tensor("wq", [D, D], F32, kind="ExternalInput").ap(),
        "wqr": nc.dram_tensor("wqr", [D, DR], F32, kind="ExternalInput").ap(),
        "wkr": nc.dram_tensor("wkr", [DC, DR], F32, kind="ExternalInput").ap(),
        "wo": nc.dram_tensor("wo", [D, D], F32, kind="ExternalInput").ap(),
        "gwT": nc.dram_tensor("gwT", [D, 8], F32, kind="ExternalInput").ap(),
        "weT": nc.dram_tensor("weT", [8, D, D], F32, kind="ExternalInput").ap(),
        "cosF": nc.dram_tensor("cosF", [DR, T], F32, kind="ExternalInput").ap(),
        "sinF": nc.dram_tensor("sinF", [DR, T], F32, kind="ExternalInput").ap(),
        "ident": nc.dram_tensor("ident", [128, 128], F32,
                                kind="ExternalInput").ap(),
        "tri": nc.dram_tensor("tri", [128, 128], F32, kind="ExternalInput").ap(),
        "out": nc.dram_tensor("out", [2048, D], F32, kind="ExternalOutput").ap(),
    }
    with contextlib.ExitStack() as ctx:
        sb = lambda n, s, d: ctx.enter_context(nc.sbuf_tensor(n, s, d))
        P = {}
        P["s"] = sb("s_res", [128, 16 * D], F32)
        P["ckv"] = sb("ckv", [128, T], BF16)
        P["q"] = sb("q_fm", [128, 8 * 2048], BF16)
        P["kr"] = sb("kr_d", [128, T], BF16)
        P["qr"] = sb("qr_d", [128, 2048], BF16)
        P["masksg"] = sb("masksg", [128, 8 * 16], F32)
        P["gwT"] = sb("gwT_sb", [128, 64], F32)
        P["negC"] = sb("negC", [128, 1], F32)
        P["zero1"] = sb("zero1", [128, 1], F32)
        P["eps"] = sb("eps_c", [128, 1], F32)
        P["ones64"] = sb("ones64", [128, 64], BF16)
        P["identf"] = sb("identf", [128, 128], F32)
        P["identb"] = sb("identb", [128, 128], BF16)
        P["trif"] = sb("trif", [128, 128], F32)
        P["trib"] = sb("trib", [128, 128], BF16)
        arena_h = sb("arena", [128, 88 * 1024], U8)

        pid_sb = ctx.enter_context(nc.sbuf_tensor("pid_sb", [1, 1], U32))

        psF = [ctx.enter_context(nc.psum_tensor("psF%d" % i, [128, 1024], F32))
               for i in range(2)]
        psC = ctx.enter_context(nc.psum_tensor("psC", [128, 512], F32))
        psD = ctx.enter_context(nc.psum_tensor("psD", [128, 512], F32))
        psum = {
            "full": [psF[0][:, :], psF[1][:, :]],
            "half": [psC[:, :], psD[:, :]],
            "pv": psC[:, :],
            "sm": psD[:, :],
            "trb": [psF[0][:, 0:64].bitcast(BF16),
                    psF[0][:, 512:576].bitcast(BF16)],
            "trf": [psF[1][:, 0:128], psF[1][:, 512:640]],
            "g8": psC[:, 0:8],
        }

        sems = {c: ctx.enter_context(nc.semaphore("sem_" + c))
                for c in CHANNELS}
        spid = ctx.enter_context(nc.semaphore("sem_pid"))

        nc.sync.dma_start(out=pid_sb[:], in_=nc.partition_id_tensor.ap()
                          ).then_inc(spid, 16)
        engines = [nc.sync, nc.tensor, nc.vector, nc.scalar, nc.gpsimd]
        idx = []
        for eng in engines:
            eng.wait_ge(spid, 16)
            val = eng.value_load(pid_sb[:1, :1])
            idx.append(val % 2)

        for i in nc.Switch(engines=engines, index=idx, n=2):
            sch = Sch()
            P["arena"] = Arena(arena_h, 88 * 1024)
            build_body(nc, sch, i, dram, P, psum)
            sch.plan()
            sch.emit(nc, sems)
    return nc


_NC_CACHE = {}


def kernel(L0_attn_x, L0_attn_W_kv_down, L0_attn_W_k_up, L0_attn_W_v_up,
           L0_attn_W_q, L0_attn_W_q_rope, L0_attn_W_k_rope, L0_attn_W_o,
           L0_ffn_gate_w, L0_ffn_expert_w):
    x = np.ascontiguousarray(np.asarray(L0_attn_x, dtype=np.float32))
    assert x.shape == (B_TOT, T, D)

    inv_freq = 1.0 / 10000.0 ** (np.arange(0, DR, 2, dtype=np.float64) / DR)
    tpos = np.arange(T, dtype=np.float64)
    freqs = np.outer(tpos, inv_freq)
    cos_t = np.cos(freqs).astype(np.float32).T
    sin_t = np.sin(freqs).astype(np.float32).T
    ident = np.eye(128, dtype=np.float32)
    kk, qq = np.meshgrid(np.arange(128), np.arange(128), indexing="ij")
    tri = (kk <= qq).astype(np.float32)

    weT = np.ascontiguousarray(
        np.asarray(L0_ffn_expert_w, dtype=np.float32).transpose(0, 2, 1))
    gwT = np.ascontiguousarray(np.asarray(L0_ffn_gate_w, dtype=np.float32).T)
    shared = {
        "wkvd": np.ascontiguousarray(np.asarray(L0_attn_W_kv_down, np.float32)),
        "wku": np.ascontiguousarray(np.asarray(L0_attn_W_k_up, np.float32)),
        "wvu": np.ascontiguousarray(np.asarray(L0_attn_W_v_up, np.float32)),
        "wq": np.ascontiguousarray(np.asarray(L0_attn_W_q, np.float32)),
        "wqr": np.ascontiguousarray(np.asarray(L0_attn_W_q_rope, np.float32)),
        "wkr": np.ascontiguousarray(np.asarray(L0_attn_W_k_rope, np.float32)),
        "wo": np.ascontiguousarray(np.asarray(L0_attn_W_o, np.float32)),
        "gwT": gwT, "weT": weT, "ident": ident, "tri": tri,
    }
    cosF, sinF = [], []
    for vpar in range(2):
        order = PERM[vpar]
        cosF.append(np.ascontiguousarray(np.concatenate(
            [np.tile(cos_t[:, 1024 * qt:1024 * qt + 1024], (2, 1))
             for qt in order], axis=1)))
        sinF.append(np.ascontiguousarray(np.concatenate(
            [np.tile(sin_t[:, 1024 * qt:1024 * qt + 1024], (2, 1))
             for qt in order], axis=1)))

    if "nc" not in _NC_CACHE:
        _NC_CACHE["nc"] = build_nc()
    nc = _NC_CACHE["nc"]

    in_maps = []
    for core in range(8):
        vpar = core % 2
        seq = core // 2
        xs = x[seq].reshape(4, 1024, D)[PERM[vpar]].reshape(T, D)
        m = dict(shared)
        m["xseq"] = np.ascontiguousarray(xs)
        m["cosF"] = cosF[vpar]
        m["sinF"] = sinF[vpar]
        in_maps.append(m)

    res = run_bass_kernel_spmd(nc, in_maps, core_ids=list(range(8)))

    out = np.empty((B_TOT, T, D), dtype=np.float32)
    for core in range(8):
        vpar = core % 2
        seq = core // 2
        oc = np.asarray(res.results[core]["out"])
        q0, q1 = PERM[vpar][0], PERM[vpar][1]
        out[seq, 1024 * q0:1024 * q0 + 1024] = oc[0:1024]
        out[seq, 1024 * q1:1024 * q1 + 1024] = oc[1024:2048]
    return out
